# version 02
# baseline (speedup 1.0000x reference)
import numpy as np
import ml_dtypes
from contextlib import ExitStack

import concourse.bass as bass
import concourse.mybir as mybir
from concourse.alu_op_type import AluOpType as OP
from concourse.bass_utils import run_bass_kernel_spmd

F32 = mybir.dt.float32
BF16 = mybir.dt.bfloat16
U8 = mybir.dt.uint8
AF = mybir.ActivationFunctionType
AX = mybir.AxisListType

B, S, H = 2, 4096, 2048
NCORES = 8
TPC = 1024
TT = 8
KF = 16
NH, HD = 16, 128
NIH = 8
NEXP, IFFN = 8, 1408
SI = 2816
SCALE = float(np.float32(1.0) / np.sqrt(np.float32(HD)))
ESHIFT = -32.0
BIG = 1.0e9
NROUND = 8
EPS = 1e-5
IBW = [512, 512, 384]
IBOFF = [0, 512, 1024]
NKT = 32

DEBUG = False


class Ctx:
    def __init__(self, nc, stack):
        self.nc = nc
        self.stack = stack
        self.sem = {}
        self.cnt = {}

    def mksem(self, name):
        self.sem[name] = self.stack.enter_context(self.nc.semaphore(name))
        self.cnt[name] = 0

    def inc(self, inst, name, n=1):
        inst.then_inc(self.sem[name], n)
        self.cnt[name] += n
        return self.cnt[name]

    def val(self, name):
        return self.cnt[name]

    def wait(self, e, name, v=None):
        if v is None:
            v = self.cnt[name]
        if v > 0:
            e.wait_ge(self.sem[name], v)


def build_program(debug=DEBUG):
    nc = bass.Bass("TRN2", target_bir_lowering=False)

    d_x = nc.dram_tensor("x_in", [TPC, H], F32, kind="ExternalInput")
    d_meta = nc.dram_tensor("meta", [1, 8], F32, kind="ExternalInput")
    d_idbf = nc.dram_tensor("id_bf", [128, 128], BF16, kind="ExternalInput")
    d_idf = nc.dram_tensor("id_f32", [128, 128], F32, kind="ExternalInput")
    d_onesbf = nc.dram_tensor("ones_bf", [128, 128], BF16, kind="ExternalInput")
    d_onesf = nc.dram_tensor("ones_f32", [1, 128], F32, kind="ExternalInput")
    d_halves = nc.dram_tensor("halves_bf", [1, 128], BF16, kind="ExternalInput")
    d_Wq = nc.dram_tensor("Wq", [H, H], F32, kind="ExternalInput")
    d_Wk = nc.dram_tensor("Wk", [H, H], F32, kind="ExternalInput")
    d_Wv = nc.dram_tensor("Wv", [H, H], F32, kind="ExternalInput")
    d_Wo = nc.dram_tensor("Wo", [H, H], F32, kind="ExternalInput")
    d_Wqi = nc.dram_tensor("Wq_ind", [H, H], F32, kind="ExternalInput")
    d_Wki = nc.dram_tensor("Wk_ind", [H, H], F32, kind="ExternalInput")
    d_hw = nc.dram_tensor("head_w", [1, 8], F32, kind="ExternalInput")
    d_temp = nc.dram_tensor("temp", [1, 1], F32, kind="ExternalInput")
    d_rw = nc.dram_tensor("router_w", [8, H], F32, kind="ExternalInput")
    d_gw = nc.dram_tensor("gate_w", [NEXP, H, IFFN], F32, kind="ExternalInput")
    d_uw = nc.dram_tensor("up_w", [NEXP, H, IFFN], F32, kind="ExternalInput")
    d_dw = nc.dram_tensor("down_w", [NEXP, IFFN, H], F32, kind="ExternalInput")
    d_sgw = nc.dram_tensor("s_gate_w", [H, SI], F32, kind="ExternalInput")
    d_suw = nc.dram_tensor("s_up_w", [H, SI], F32, kind="ExternalInput")
    d_sdw = nc.dram_tensor("s_down_w", [SI, H], F32, kind="ExternalInput")
    d_sgb = nc.dram_tensor("s_gate_b", [1, SI], F32, kind="ExternalInput")
    d_sub = nc.dram_tensor("s_up_b", [1, SI], F32, kind="ExternalInput")
    d_sdb = nc.dram_tensor("s_down_b", [1, H], F32, kind="ExternalInput")
    d_out = nc.dram_tensor("x_out", [TPC, H], F32, kind="ExternalOutput")

    CC1R = 6144
    d_cc1i = nc.dram_tensor("cc1i", [CC1R, TPC], BF16)
    d_cc1o = nc.dram_tensor("cc1o", [4 * CC1R, TPC], BF16)
    d_cc2i = nc.dram_tensor("cc2i", [1, TPC], F32)
    d_cc2o = nc.dram_tensor("cc2o", [4, TPC], F32)
    d_qsp = nc.dram_tensor("qspill", [H, TPC], BF16)

    dbg = {}
    if debug:
        dbg["x1"] = nc.dram_tensor("dbg_x1", [TPC, H], F32, kind="ExternalOutput")
        dbg["rel"] = nc.dram_tensor("dbg_rel", [4, TPC], F32, kind="ExternalOutput")
        dbg["sel"] = nc.dram_tensor("dbg_sel", [128, NKT], F32, kind="ExternalOutput")
        dbg["x2"] = nc.dram_tensor("dbg_x2", [TPC, H], F32, kind="ExternalOutput")

    main = ExitStack()
    with main:
        c = Ctx(nc, main)
        sb = lambda st, *a: st.enter_context(nc.sbuf_tensor(*a))
        psn = lambda *a: main.enter_context(nc.psum_tensor(*a))
        semnames = (["tS", "vS", "aS", "gS", "ioS", "ccS", "ldS", "bbS", "cbS",
                     "qsS", "rlS", "r4S", "outS"]
                    + [f"wgS{i}" for i in range(5)] + [f"wuS{i}" for i in range(5)]
                    + [f"wdS{i}" for i in range(2)] + [f"stS{i}" for i in range(2)])
        for sname in semnames:
            c.mksem(sname)

        s_x = sb(main, "s_x", [128, TT, H], F32)
        s_idbf = sb(main, "s_idbf", [128, 128], BF16)
        s_idf = sb(main, "s_idf", [128, 128], F32)
        s_onesbf = sb(main, "s_onesbf", [128, 128], BF16)
        s_onesf = sb(main, "s_onesf", [1, 128], F32)
        s_halves = sb(main, "s_halves", [1, 128], BF16)
        s_meta = sb(main, "s_meta", [1, 8], F32)
        s_hw = sb(main, "s_hw", [1, 8], F32)
        s_temp = sb(main, "s_temp", [1, 1], F32)
        s_stats = sb(main, "s_stats", [128, 6, TT], F32)
        s_xn_tm = sb(main, "s_xn_tm", [128, 1, H], F32)
        s_stage = sb(main, "s_stage", [128, 4, 128], F32)
        s_rwTf = sb(main, "s_rwTf", [128, KF, 8], F32)
        s_lgtm = sb(main, "s_lgtm", [128, TT, 8], F32)
        s_r2 = sb(main, "s_r2", [128, 8, TT], F32)
        s_ce = sb(main, "s_ce", [128, TT, 8], F32)
        s_wg = sb(main, "s_wg", [128, 5, 4, 512], BF16)
        s_sg = sb(main, "s_sg", [128, 2, 512], BF16)
        s_relpc = sb(main, "s_relpc", [128, NKT], F32)
        s_sel = sb(main, "s_sel", [128, NKT], F32)
        s_bis = sb(main, "s_bis", [1, 16], F32)
        s_smallc = sb(main, "s_smallc", [128, 8], F32)

        pb = [psn(f"pb{i}", [128, 512], F32) for i in range(7)]
        ptb = psn("ptb", [128, 1024], BF16)

        block = main.enter_context(nc.Block())
        sec = lambda name: getattr(block, name)

        def snap():
            return {k: c.val(k) for k in c.cnt}

        def soft_barrier():
            bar = snap()
            for en in ("tensor", "vector", "scalar", "gpsimd", "sync"):
                @sec(en)
                def _(e, bar=bar):
                    for k, v in bar.items():
                        c.wait(e, k, v)

        RGfree = {i: 0 for i in range(5)}
        RGload = {}

        @sec("sync")
        def _(e):
            c.inc(e.dma_start(s_x[:], d_x.rearrange("(tt p) f -> p tt f", p=128)), "ldS", 16)
            c.inc(e.dma_start(s_idbf[:], d_idbf[:]), "ldS", 16)
            c.inc(e.dma_start(s_idf[:], d_idf[:]), "ldS", 16)
            c.inc(e.dma_start(s_onesbf[:], d_onesbf[:]), "ldS", 16)
            c.inc(e.dma_start(s_onesf[:], d_onesf[:]), "ldS", 16)
            c.inc(e.dma_start(s_meta[:], d_meta[:]), "ldS", 16)
            c.inc(e.dma_start(s_hw[:], d_hw[:]), "ldS", 16)
            c.inc(e.dma_start(s_temp[:], d_temp[:]), "ldS", 16)
            c.inc(e.dma_start(s_xn_tm[0:8, 0, :], d_rw[:]), "ldS", 16)
            c.inc(e.dma_start(s_halves[:], d_halves[:]), "ldS", 16)
        lds_done = c.val("ldS")

        for kf in range(KF):
            bank = pb[4 + (kf % 2)]
            prev_v = c.val("vS")

            @sec("tensor")
            def _(e, kf=kf, bank=bank, prev_v=prev_v):
                if kf == 0:
                    c.wait(e, "ldS", lds_done)
                c.wait(e, "vS", prev_v)
                c.inc(e.transpose(bank[:, 0:8], s_xn_tm[0:8, 0, kf * 128:(kf + 1) * 128],
                                  s_idf[0:8, 0:8]), "tS")
            tdone = c.val("tS")

            @sec("vector")
            def _(e, kf=kf, bank=bank, tdone=tdone):
                c.wait(e, "tS", tdone)
                c.inc(e.tensor_copy(s_rwTf[:, kf, :], bank[:, 0:8]), "vS")
        rwT_done = c.val("vS")

        def layernorm_to_xnT(xnT, sq_junk, xwait=None, with_logits=False):
            @sec("scalar")
            def _(e):
                c.wait(e, "ldS", lds_done)
                if xwait:
                    c.wait(e, xwait[0], xwait[1])
                c.wait(e, "vS", c.val("vS"))
                for tt in range(TT):
                    c.wait(e, "aS", c.val("aS"))
                    c.inc(e.activation(sq_junk, s_x[:, tt, :], AF.Square,
                                       accum_out=s_stats[:, 1, tt:tt + 1]), "aS")
            sq_done = c.val("aS")

            @sec("vector")
            def _(e):
                c.wait(e, "ldS", lds_done)
                if xwait:
                    c.wait(e, xwait[0], xwait[1])
                for tt in range(TT):
                    c.inc(e.reduce_sum(s_stats[:, 0, tt:tt + 1], s_x[:, tt, :], axis=AX.X), "vS")
                c.wait(e, "vS")
                c.wait(e, "aS", sq_done)
                c.inc(e.tensor_scalar(s_stats[:, 2, :], s_stats[:, 0, :], 1.0 / H, None, op0=OP.mult), "vS")
                c.wait(e, "vS")
                c.inc(e.tensor_tensor(s_stats[:, 3, :], s_stats[:, 2, :], s_stats[:, 2, :], op=OP.mult), "vS")
                c.wait(e, "vS")
                c.inc(e.scalar_tensor_tensor(s_stats[:, 3, :], s_stats[:, 1, :], 1.0 / H,
                                             s_stats[:, 3, :], op0=OP.mult, op1=OP.subtract), "vS")
                c.wait(e, "vS")
                c.inc(e.tensor_scalar(s_stats[:, 3, :], s_stats[:, 3, :], EPS, None, op0=OP.add), "vS")
            varpe_done = c.val("vS")

            @sec("scalar")
            def _(e):
                c.wait(e, "vS", varpe_done)
                c.inc(e.activation(s_stats[:, 4, :], s_stats[:, 3, :], AF.Ln), "aS")
                c.wait(e, "aS")
                c.inc(e.activation(s_stats[:, 5, :], s_stats[:, 4, :], AF.Exp, scale=-0.5), "aS")
            rstd_done = c.val("aS")

            for tt in range(TT):
                prev_t = c.val("tS")

                @sec("vector")
                def _(e, tt=tt, prev_t=prev_t):
                    c.wait(e, "aS", rstd_done)
                    c.wait(e, "tS", prev_t)
                    c.inc(e.tensor_scalar(s_xn_tm[:, 0, :], s_x[:, tt, :],
                                          s_stats[:, 2, tt:tt + 1], s_stats[:, 5, tt:tt + 1],
                                          op0=OP.subtract, op1=OP.mult), "vS")
                xn_ready = c.val("vS")

                for half in range(2):
                    bank = pb[4 + half]
                    for grp in range(2):
                        prev_v = c.val("vS")

                        @sec("tensor")
                        def _(e, half=half, grp=grp, bank=bank, w=max(xn_ready, prev_v)):
                            c.wait(e, "vS", w)
                            for j in range(4):
                                kf = half * 8 + grp * 4 + j
                                c.inc(e.transpose(bank[:, j * 128:(j + 1) * 128],
                                                  s_xn_tm[:, 0, kf * 128:(kf + 1) * 128],
                                                  s_idf[:]), "tS")
                        tdone = c.val("tS")

                        @sec("vector")
                        def _(e, tt=tt, half=half, grp=grp, bank=bank, tdone=tdone,
                              with_logits=with_logits):
                            c.wait(e, "tS", tdone)
                            for j in range(4):
                                kf = half * 8 + grp * 4 + j
                                c.inc(e.tensor_copy(xnT[:, kf, tt * 128:(tt + 1) * 128],
                                                    bank[:, j * 128:(j + 1) * 128]), "vS")
                            if with_logits:
                                for j in range(4):
                                    c.inc(e.tensor_copy(s_stage[:, j, :],
                                                        bank[:, j * 128:(j + 1) * 128]), "vS")
                        vdone = c.val("vS")

                        if with_logits:
                            @sec("tensor")
                            def _(e, half=half, grp=grp, vdone=vdone):
                                c.wait(e, "vS", vdone)
                                for j in range(4):
                                    kf = half * 8 + grp * 4 + j
                                    c.inc(e.matmul(pb[6][:, 0:8], s_stage[:, j, :],
                                                   s_rwTf[:, kf, :],
                                                   start=(kf == 0), stop=(kf == KF - 1)), "tS")
                if with_logits:
                    lg_mm_done = c.val("tS")

                    @sec("vector")
                    def _(e, tt=tt, lg_mm_done=lg_mm_done):
                        c.wait(e, "tS", lg_mm_done)
                        c.inc(e.tensor_copy(s_lgtm[:, tt, :], pb[6][:, 0:8]), "vS")
            return c.val("vS")

        def top2():
            @sec("vector")
            def _(e):
                c.wait(e, "vS", c.val("vS"))
                for tt in range(TT):
                    c.inc(e.reduce_max(s_r2[:, 0, tt:tt + 1], s_lgtm[:, tt, :], axis=AX.X), "vS")
                c.wait(e, "vS")
                for tt in range(TT):
                    c.inc(e.tensor_scalar(s_ce[:, tt, :], s_lgtm[:, tt, :],
                                          s_r2[:, 0, tt:tt + 1], -BIG, op0=OP.is_ge, op1=OP.mult), "vS")
                c.wait(e, "vS")
                for tt in range(TT):
                    c.inc(e.tensor_tensor(s_ce[:, tt, :], s_lgtm[:, tt, :], s_ce[:, tt, :], op=OP.add), "vS")
                c.wait(e, "vS")
                for tt in range(TT):
                    c.inc(e.reduce_max(s_r2[:, 1, tt:tt + 1], s_ce[:, tt, :], axis=AX.X), "vS")
                c.wait(e, "vS")
                c.inc(e.tensor_tensor(s_r2[:, 2, :], s_r2[:, 1, :], s_r2[:, 0, :], op=OP.subtract), "vS")
            d_done = c.val("vS")

            @sec("scalar")
            def _(e):
                c.wait(e, "vS", d_done)
                c.inc(e.activation(s_r2[:, 3, :], s_r2[:, 2, :], AF.Exp), "aS")
            ed_done = c.val("aS")

            @sec("vector")
            def _(e):
                c.wait(e, "aS", ed_done)
                c.inc(e.tensor_scalar(s_r2[:, 4, :], s_r2[:, 3, :], 1.0, None, op0=OP.add), "vS")
                c.wait(e, "vS")
                c.inc(e.reciprocal(s_r2[:, 4, :], s_r2[:, 4, :]), "vS")
                c.wait(e, "vS")
                c.inc(e.tensor_scalar(s_r2[:, 4, :], s_r2[:, 4, :], 0.5, None, op0=OP.mult), "vS")
                c.wait(e, "vS")
                c.inc(e.tensor_tensor(s_r2[:, 5, :], s_r2[:, 3, :], s_r2[:, 4, :], op=OP.mult), "vS")
                c.wait(e, "vS")
                for tt in range(TT):
                    c.inc(e.tensor_scalar(s_ce[:, tt, :], s_lgtm[:, tt, :],
                                          s_r2[:, 0, tt:tt + 1], s_r2[:, 4, tt:tt + 1],
                                          op0=OP.is_equal, op1=OP.mult), "vS")
                c.wait(e, "vS")
                for tt in range(TT):
                    c.inc(e.tensor_scalar(s_lgtm[:, tt, :], s_lgtm[:, tt, :],
                                          s_r2[:, 1, tt:tt + 1], s_r2[:, 5, tt:tt + 1],
                                          op0=OP.is_equal, op1=OP.mult), "vS")
                c.wait(e, "vS")
                for tt in range(TT):
                    c.inc(e.tensor_tensor(s_ce[:, tt, :], s_ce[:, tt, :], s_lgtm[:, tt, :], op=OP.add), "vS")
            return c.val("vS")

        def moe(fb_, ce_done, xnT_done):
            xnT = fb_["xnT"]
            s_wu, s_wd, s_act = fb_["wu"], fb_["wd"], fb_["act"]
            s_sgbb, s_subb, s_sdbb = fb_["sgb"], fb_["sub"], fb_["sdb"]
            bias_done = fb_["bias_done"]
            jobs = [("exp", i) for i in range(NEXP)] + [("sh", 0), ("sh", 1)]
            ru_free = {i: 0 for i in range(5)}
            ru_load = {}
            wd_free = [0, 0]
            wd_load = {}
            psets = [(pb[0], pb[1]), (pb[2], pb[3]), (pb[4], pb[5])]
            bank_free = [c.val("vS")] * 6
            sweep_i = 0
            hidx = 0

            def gu_src(kind, idx, mat, r0, c0, w):
                if kind == "exp":
                    t = d_gw if mat == "g" else d_uw
                    return t[idx, r0:r0 + 512, c0:c0 + w].rearrange("(kf p) i -> p kf i", p=128)
                base = idx * IFFN
                t = d_sgw if mat == "g" else d_suw
                return t[r0:r0 + 512, base + c0:base + c0 + w].rearrange("(kf p) i -> p kf i", p=128)

            def dw_src(kind, idx, fbk):
                if kind == "exp":
                    return d_dw[idx, :, fbk * 256:(fbk + 1) * 256].rearrange("(ic p) f -> p ic f", p=128)
                base = idx * IFFN
                return d_sdw[base:base + IFFN, fbk * 256:(fbk + 1) * 256].rearrange("(ic p) f -> p ic f", p=128)

            for kind, idx in jobs:
                for ibp in range(3):
                    w = IBW[ibp]
                    c0 = IBOFF[ibp]
                    my_h = [hidx + i for i in range(4)]
                    hidx += 4

                    @sec("gpsimd")
                    def _(e, kind=kind, idx=idx, c0=c0, w=w, my_h=my_h,
                          gfree=[RGfree[hh % 5] for hh in my_h],
                          ufree=[ru_free[hh % 5] for hh in my_h]):
                        for i, hh in enumerate(my_h):
                            r0 = i * 512
                            buf = hh % 5
                            c.wait(e, "tS", gfree[i])
                            c.inc(e.dma_start(s_wg[:, buf, :, 0:w], gu_src(kind, idx, "g", r0, c0, w)), f"wgS{buf}", 16)
                            RGload[("m", hh)] = (f"wgS{buf}", c.val(f"wgS{buf}"))
                            c.wait(e, "tS", ufree[i])
                            c.inc(e.dma_start(s_wu[:, buf, :, 0:w], gu_src(kind, idx, "u", r0, c0, w)), f"wuS{buf}", 16)
                            ru_load[hh] = (f"wuS{buf}", c.val(f"wuS{buf}"))

                    for tt in range(TT):
                        si = sweep_i % 3
                        sweep_i += 1
                        pg, pu = psets[si]
                        pfree = max(bank_free[2 * si], bank_free[2 * si + 1])

                        @sec("tensor")
                        def _(e, kind=kind, idx=idx, tt=tt, w=w, c0=c0, pg=pg, pu=pu,
                              pfree=pfree, my_h=my_h):
                            c.wait(e, "vS", max(pfree, ce_done, xnT_done))
                            for kf in range(KF):
                                hh = my_h[kf // 4]
                                buf = hh % 5
                                if kf % 4 == 0:
                                    c.wait(e, *RGload[("m", hh)])
                                    c.wait(e, *ru_load[hh])
                                last = (kf == KF - 1) and (kind == "exp")
                                c.inc(e.matmul(pg[:, 0:w], xnT[:, kf, tt * 128:(tt + 1) * 128],
                                               s_wg[:, buf, kf % 4, 0:w],
                                               start=(kf == 0), stop=last), "tS")
                                c.inc(e.matmul(pu[:, 0:w], xnT[:, kf, tt * 128:(tt + 1) * 128],
                                               s_wu[:, buf, kf % 4, 0:w],
                                               start=(kf == 0), stop=last), "tS")
                                if kf % 4 == 3:
                                    RGfree[buf] = c.val("tS") - 1
                                    ru_free[buf] = c.val("tS")
                            if kind == "sh":
                                base = idx * IFFN
                                c.wait(e, "bbS", bias_done)
                                c.inc(e.matmul(pg[:, 0:w], s_onesbf[0:1, :],
                                               s_sgbb[0:1, base + c0:base + c0 + w],
                                               start=False, stop=True), "tS")
                                c.inc(e.matmul(pu[:, 0:w], s_onesbf[0:1, :],
                                               s_subb[0:1, base + c0:base + c0 + w],
                                               start=False, stop=True), "tS")
                        mm_done = c.val("tS")

                        prev_stt = c.val("vS")

                        @sec("scalar")
                        def _(e, tt=tt, w=w, pg=pg, mm_done=mm_done, prev_stt=prev_stt):
                            c.wait(e, "tS", mm_done)
                            c.wait(e, "vS", prev_stt)
                            c.inc(e.activation(s_sg[:, tt % 2, 0:w], pg[:, 0:w], AF.Silu), "aS")
                        silu_done = c.val("aS")

                        @sec("vector")
                        def _(e, kind=kind, idx=idx, tt=tt, w=w, c0=c0, pu=pu,
                              mm_done=mm_done, silu_done=silu_done):
                            c.wait(e, "tS", mm_done)
                            c.wait(e, "aS", silu_done)
                            scalar = s_ce[:, tt, idx:idx + 1] if kind == "exp" else 0.5
                            c.inc(e.scalar_tensor_tensor(s_act[:, tt, c0:c0 + w], pu[:, 0:w],
                                                         scalar, s_sg[:, tt % 2, 0:w],
                                                         op0=OP.mult, op1=OP.mult), "vS")
                        bank_free[2 * si] = bank_free[2 * si + 1] = c.val("vS")
                act_done = c.val("vS")

                for tt in range(TT):
                    for grp, gn in ((0, 4), (1, 4), (2, 3)):
                        prev_v = c.val("vS")

                        @sec("tensor")
                        def _(e, tt=tt, grp=grp, gn=gn, w=max(prev_v, act_done)):
                            c.wait(e, "vS", w)
                            for j in range(gn):
                                ic = grp * 4 + j
                                sl = ic % 8
                                c.inc(e.transpose(ptb[:, sl * 128:(sl + 1) * 128],
                                                  s_act[:, tt, ic * 128:(ic + 1) * 128],
                                                  s_idbf[:]), "tS")
                        tdone = c.val("tS")

                        @sec("vector")
                        def _(e, tt=tt, grp=grp, gn=gn, tdone=tdone):
                            c.wait(e, "tS", tdone)
                            for j in range(gn):
                                ic = grp * 4 + j
                                sl = ic % 8
                                c.inc(e.tensor_copy(s_act[:, tt, ic * 128:(ic + 1) * 128],
                                                    ptb[:, sl * 128:(sl + 1) * 128]), "vS")
                actT_done = c.val("vS")

                for fbk in range(8):
                    buf = fbk % 2

                    @sec("gpsimd")
                    def _(e, kind=kind, idx=idx, fbk=fbk, buf=buf, wfree=wd_free[buf]):
                        c.wait(e, "tS", wfree)
                        c.inc(e.dma_start(s_wd[:, buf, :, :], dw_src(kind, idx, fbk)), f"wdS{buf}", 16)
                        wd_load[fbk] = (f"wdS{buf}", c.val(f"wdS{buf}"))

                    for tt in range(TT):
                        bi = (fbk * TT + tt) % 6
                        bank = pb[bi]
                        bfree = bank_free[bi]

                        @sec("tensor")
                        def _(e, kind=kind, idx=idx, tt=tt, fbk=fbk, buf=buf, bank=bank,
                              bfree=bfree, actT_done=actT_done):
                            c.wait(e, "vS", max(bfree, actT_done))
                            c.wait(e, *wd_load[fbk])
                            add_bias = (kind == "sh" and idx == 1)
                            for ic in range(11):
                                last = (ic == 10) and not add_bias
                                c.inc(e.matmul(bank[:, 0:256], s_act[:, tt, ic * 128:(ic + 1) * 128],
                                               s_wd[:, buf, ic, :],
                                               start=(ic == 0), stop=last), "tS")
                            if add_bias:
                                c.wait(e, "bbS", bias_done)
                                c.inc(e.matmul(bank[:, 0:256], s_halves[0:1, :],
                                               s_sdbb[0:1, fbk * 256:(fbk + 1) * 256],
                                               start=False, stop=True), "tS")
                            if tt == TT - 1:
                                wd_free[buf] = c.val("tS")
                        dmm_done = c.val("tS")

                        @sec("vector")
                        def _(e, tt=tt, fbk=fbk, bank=bank, dmm_done=dmm_done):
                            c.wait(e, "tS", dmm_done)
                            c.inc(e.tensor_tensor(s_x[:, tt, fbk * 256:(fbk + 1) * 256],
                                                  bank[:, 0:256], s_x[:, tt, fbk * 256:(fbk + 1) * 256],
                                                  op=OP.add), "vS")
                        bank_free[bi] = c.val("vS")
            return c.val("vS")

        wincnt = [0]

        def open_ffn_window(st):
            wincnt[0] += 1
            sfx = wincnt[0]
            fb_ = {}
            fb_["xnT"] = sb(st, f"s_xnT_{sfx}", [128, KF, TPC], BF16)
            fb_["wu"] = sb(st, f"s_wu_{sfx}", [128, 5, 4, 512], BF16)
            fb_["wd"] = sb(st, f"s_wd_{sfx}", [128, 2, 11, 256], BF16)
            fb_["act"] = sb(st, f"s_act_{sfx}", [128, TT, IFFN], BF16)
            fb_["sgb"] = sb(st, f"s_sgbb_{sfx}", [1, SI], BF16)
            fb_["sub"] = sb(st, f"s_subb_{sfx}", [1, SI], BF16)
            fb_["sdb"] = sb(st, f"s_sdbb_{sfx}", [1, H], BF16)

            @sec("gpsimd")
            def _(e, fb_=fb_):
                c.inc(e.dma_start(fb_["sgb"][:], d_sgb[:]), "bbS", 16)
                c.inc(e.dma_start(fb_["sub"][:], d_sub[:]), "bbS", 16)
                c.inc(e.dma_start(fb_["sdb"][:], d_sdb[:]), "bbS", 16)
            fb_["bias_done"] = c.val("bbS")
            fb_["sq_junk"] = fb_["act"].rearrange("p a b -> p (a b)")[:, 0:H]
            return fb_

        w2 = ExitStack()
        fb1 = open_ffn_window(w2)
        xnT1_done = layernorm_to_xnT(fb1["xnT"], fb1["sq_junk"],
                                     xwait=("vS", rwT_done), with_logits=True)
        ce_done = top2()
        x1_done = moe(fb1, ce_done, xnT1_done)
        if debug:
            @sec("sync")
            def _(e):
                c.wait(e, "vS", x1_done)
                c.inc(e.dma_start(dbg["x1"].rearrange("(tt p) f -> p tt f", p=128), s_x[:]), "outS", 16)

        fb1_xnT = fb1["xnT"]
        w2.close()
        soft_barrier()

        w3 = ExitStack()
        fm1 = sb(w3, "s_fm1", [128, KF, TPC], BF16)
        fm2 = sb(w3, "s_fm2", [128, KF, TPC], BF16)
        s_acc = sb(w3, "s_acc", [128, 64, 8], F32)
        s_accr = sb(w3, "s_accr", [128, 64], F32)
        s_hw64 = sb(w3, "s_hw64", [128, 64], F32)
        s_relc = sb(w3, "s_relc", [128, 8], F32)
        s_relrow = sb(w3, "s_relrow", [8, 128], F32)
        w1 = ExitStack()
        xnT2 = sb(w1, "s_xnT2", [128, KF, TPC], BF16)

        sqj2 = fm1.rearrange("p a b -> p (a b)")[:, 0:H]
        xnT2_done = layernorm_to_xnT(xnT2, sqj2, xwait=("vS", x1_done), with_logits=False)

        phidx = [120]

        def tm_sweep(name, wsrc, lhsT_buf, lhsT_done, evac_fn, nco=4, dst_wait=()):
            psets = [pb[0], pb[1], pb[2]]
            pfree = [c.val("vS")] * 3
            swp = [0]
            for coln in range(nco):
                my_h = [phidx[0] + i for i in range(4)]
                phidx[0] += 4

                @sec("gpsimd")
                def _(e, coln=coln, my_h=my_h, gfree=[RGfree[hh % 5] for hh in my_h]):
                    for i, hh in enumerate(my_h):
                        r0 = i * 512
                        buf = hh % 5
                        c.wait(e, "tS", gfree[i])
                        c.inc(e.dma_start(
                            s_wg[:, buf, :, :],
                            wsrc[r0:r0 + 512, coln * 512:(coln + 1) * 512]
                            .rearrange("(kf p) i -> p kf i", p=128)), f"wgS{buf}", 16)
                        RGload[("m", hh)] = (f"wgS{buf}", c.val(f"wgS{buf}"))

                for tt in range(TT):
                    si = swp[0] % 3
                    swp[0] += 1
                    pg = psets[si]
                    pf = pfree[si]

                    @sec("tensor")
                    def _(e, tt=tt, coln=coln, pg=pg, pf=pf, my_h=my_h, lhsT_done=lhsT_done):
                        c.wait(e, "vS", max(pf, lhsT_done))
                        for kf in range(KF):
                            hh = my_h[kf // 4]
                            buf = hh % 5
                            if kf % 4 == 0:
                                c.wait(e, *RGload[("m", hh)])
                            c.inc(e.matmul(pg[:], lhsT_buf[:, kf, tt * 128:(tt + 1) * 128],
                                           s_wg[:, buf, kf % 4, :],
                                           start=(kf == 0), stop=(kf == KF - 1)), "tS")
                            if kf % 4 == 3:
                                RGfree[buf] = c.val("tS")
                    mm_done = c.val("tS")

                    @sec("vector")
                    def _(e, tt=tt, coln=coln, pg=pg, mm_done=mm_done, dst_wait=dst_wait):
                        c.wait(e, "tS", mm_done)
                        for dw_ in dst_wait:
                            c.wait(e, dw_[0], dw_[1])
                        evac_fn(e, tt, coln, pg)
                    pfree[si] = c.val("vS")
            return c.val("vS")

        def tm_store(dst):
            dv = dst.rearrange("p a b -> p (a b)").rearrange("p (tt f) -> p tt f", tt=TT)

            def f(e, tt, coln, psum):
                c.inc(e.tensor_copy(dv[:, tt, coln * 512:(coln + 1) * 512], psum[:]), "vS")
            return f

        def transpose_tm_to_fm(src, dst, src_done, dst_wait=()):
            sv = src.rearrange("p a b -> p (a b)").rearrange("p (tt f) -> p tt f", tt=TT)
            for tt in range(TT):
                for grp in range(4):
                    prev_v = c.val("vS")

                    @sec("tensor")
                    def _(e, tt=tt, grp=grp, w=max(prev_v, src_done)):
                        c.wait(e, "vS", w)
                        for j in range(4):
                            kf = grp * 4 + j
                            sl = (grp % 2) * 4 + j
                            c.inc(e.transpose(ptb[:, sl * 128:(sl + 1) * 128],
                                              sv[:, tt, kf * 128:(kf + 1) * 128],
                                              s_idbf[:]), "tS")
                    tdone = c.val("tS")

                    @sec("vector")
                    def _(e, tt=tt, grp=grp, tdone=tdone, dst_wait=dst_wait):
                        c.wait(e, "tS", tdone)
                        for dw_ in dst_wait:
                            c.wait(e, dw_[0], dw_[1])
                        for j in range(4):
                            kf = grp * 4 + j
                            sl = (grp % 2) * 4 + j
                            c.inc(e.tensor_copy(dst[:, kf, tt * 128:(tt + 1) * 128],
                                                ptb[:, sl * 128:(sl + 1) * 128]), "vS")
            return c.val("vS")

        vtm_done = tm_sweep("v", d_Wv, xnT2, xnT2_done, tm_store(fm1))

        @sec("sync")
        def _(e):
            c.wait(e, "vS", vtm_done)
            fv = fm1.rearrange("p (tt j) f -> p tt j f", tt=TT)
            for j in range(2):
                c.inc(e.dma_start(
                    d_cc1i[4096 + j * 1024:4096 + (j + 1) * 1024, :]
                    .rearrange("(tt p) f -> p tt f", p=128),
                    fv[:, :, j, :]), "cbS", 16)
        v_sent = c.val("cbS")

        ktm_done = tm_sweep("k", d_Wk, xnT2, xnT2_done, tm_store(fm1),
                            dst_wait=(("cbS", v_sent),))
        kfm_done = transpose_tm_to_fm(fm1, fm2, ktm_done)

        @sec("sync")
        def _(e):
            c.wait(e, "vS", kfm_done)
            c.inc(e.dma_start(d_cc1i[0:2048, :].rearrange("(kf p) t -> p kf t", p=128),
                              fm2[:]), "cbS", 16)
        k_sent = c.val("cbS")

        kptm_done = tm_sweep("kp", d_Wki, fm2, kfm_done, tm_store(fm1))
        kpfm_done = transpose_tm_to_fm(fm1, fm2, kptm_done,
                                       dst_wait=(("cbS", k_sent),))

        @sec("sync")
        def _(e):
            c.wait(e, "vS", kpfm_done)
            c.inc(e.dma_start(d_cc1i[2048:4096, :].rearrange("(kf p) t -> p kf t", p=128),
                              fm2[:]), "cbS", 16)
        kp_sent = c.val("cbS")

        qtm_done = tm_sweep("q", d_Wq, xnT2, xnT2_done, tm_store(fm1))
        qfm_done = transpose_tm_to_fm(fm1, fm2, qtm_done,
                                      dst_wait=(("cbS", kp_sent),))

        @sec("sync")
        def _(e):
            c.wait(e, "vS", qfm_done)
            c.inc(e.dma_start(d_qsp.rearrange("(kf p) t -> p kf t", p=128), fm2[:]), "qsS", 16)
        q_sent = c.val("qsS")

        qptm_done = tm_sweep("qp", d_Wqi, fm2, qfm_done, tm_store(fm1))
        @sec("vector")
        def _(e):
            c.wait(e, "qsS", q_sent)
            c.inc(e.tensor_copy(s_stats[0:1, 0, 0:1], s_stats[0:1, 0, 0:1]), "vS")
        qp_fence = c.val("vS")
        qpfm_done = transpose_tm_to_fm(fm1, fm2, max(qptm_done, qp_fence))

        w1.close()

        @sec("gpsimd")
        def _(e):
            c.wait(e, "cbS", max(v_sent, k_sent, kp_sent))
            c.inc(e.collective_compute(
                "AllGather", OP.bypass,
                ins=[d_cc1i.ap().opt()], outs=[d_cc1o.ap().opt()],
                replica_groups=[[0, 1, 2, 3], [4, 5, 6, 7]]), "ccS")
        cc1_done = c.val("ccS")

        @sec("tensor")
        def _(e):
            c.wait(e, "ldS", lds_done)
            c.wait(e, "vS", c.val("vS"))
            c.inc(e.matmul(pb[6][:, 0:8], s_onesf[:], s_hw[:], start=True, stop=True), "tS")
        hwb_mm = c.val("tS")

        @sec("vector")
        def _(e, hwb_mm=hwb_mm):
            c.wait(e, "tS", hwb_mm)
            for qt in range(TT):
                c.inc(e.tensor_copy(s_hw64[:, qt * 8:(qt + 1) * 8], pb[6][:, 0:8]), "vS")
        hw64_done = c.val("vS")

        ind_junk = fm1.rearrange("p a b -> p (a b)")
        pfree_ind = [("vS", 0)] * 4
        ev_i = [0]
        for ih in range(NIH):
            for kb in range(8):
                my_h = [phidx[0], phidx[0] + 1]
                phidx[0] += 2
                rr, tl0 = kb // 2, (kb % 2) * 512

                @sec("gpsimd")
                def _(e, ih=ih, kb=kb, rr=rr, tl0=tl0, my_h=my_h,
                      gfree=[RGfree[hh % 5] for hh in my_h], cc1_done=cc1_done):
                    c.wait(e, "ccS", cc1_done)
                    for i, hh in enumerate(my_h):
                        buf = hh % 5
                        c.wait(e, "tS", gfree[i])
                        c.inc(e.dma_start(
                            s_wg[:, buf, 0, :],
                            d_cc1o[rr * CC1R + 2048 + ih * 256 + i * 128:
                                   rr * CC1R + 2048 + ih * 256 + i * 128 + 128,
                                   tl0:tl0 + 512]), f"wgS{buf}", 16)
                        RGload[("m", hh)] = (f"wgS{buf}", c.val(f"wgS{buf}"))

                for qt in range(TT):
                    bi = ev_i[0] % 4
                    bank = pb[bi]
                    pf = pfree_ind[bi]

                    @sec("tensor")
                    def _(e, ih=ih, kb=kb, qt=qt, bank=bank, pf=pf, my_h=my_h,
                          qpfm_done=qpfm_done):
                        if qt == 0:
                            c.wait(e, *RGload[("m", my_h[0])])
                            c.wait(e, *RGload[("m", my_h[1])])
                            c.wait(e, "vS", qpfm_done)
                        c.wait(e, pf[0], pf[1])
                        for i, hh in enumerate(my_h):
                            buf = hh % 5
                            c.inc(e.matmul(bank[:], fm2[:, ih * 2 + i, qt * 128:(qt + 1) * 128],
                                           s_wg[:, buf, 0, :],
                                           start=(i == 0), stop=(i == 1)), "tS")
                        if qt == TT - 1:
                            for hh in my_h:
                                RGfree[hh % 5] = c.val("tS")
                    imm_done = c.val("tS")

                    acc_ap = s_acc[:, qt * 8 + ih, kb:kb + 1]
                    if ev_i[0] % 2 == 0:
                        @sec("scalar")
                        def _(e, bank=bank, acc_ap=acc_ap, imm_done=imm_done):
                            c.wait(e, "tS", imm_done)
                            c.wait(e, "aS", c.val("aS"))
                            c.inc(e.activation(ind_junk[:, 0:512], bank[:], AF.Relu,
                                               accum_out=acc_ap), "aS")
                        pfree_ind[bi] = ("aS", c.val("aS"))
                    else:
                        @sec("vector")
                        def _(e, bank=bank, acc_ap=acc_ap, imm_done=imm_done):
                            c.wait(e, "tS", imm_done)
                            c.wait(e, "vS", c.val("vS"))
                            c.inc(e.tensor_scalar(ind_junk[:, 512:1024], bank[:], 0.0, 0.0,
                                                  op0=OP.max, op1=OP.add, accum_out=acc_ap), "vS")
                        pfree_ind[bi] = ("vS", c.val("vS"))
                    ev_i[0] += 1

        ind_mm_all = c.val("tS")
        ind_ev_a = c.val("aS")
        ind_ev_v = c.val("vS")

        @sec("vector")
        def _(e):
            c.wait(e, "aS", ind_ev_a)
            c.wait(e, "vS", ind_ev_v)
            c.wait(e, "vS", hw64_done)
            c.inc(e.tensor_reduce(s_accr[:], s_acc[:], axis=AX.X, op=OP.add), "vS")
            c.wait(e, "vS")
            c.inc(e.tensor_tensor(s_accr[:], s_accr[:], s_hw64[:], op=OP.mult), "vS")
            c.wait(e, "vS")
            c.inc(e.tensor_reduce(s_relc[:], s_accr.rearrange("p (qt ih) -> p qt ih", qt=TT),
                                  axis=AX.X, op=OP.add), "vS")
        relc_raw = c.val("vS")

        @sec("scalar")
        def _(e):
            c.wait(e, "ldS", lds_done)
            c.inc(e.activation(s_temp[:], s_temp[:], AF.Exp, scale=-1.0), "aS")
        et_done = c.val("aS")

        @sec("tensor")
        def _(e, et_done=et_done):
            c.wait(e, "aS", et_done)
            c.wait(e, "vS", c.val("vS"))
            c.inc(e.matmul(pb[5][:, 0:1], s_onesf[:], s_temp[:], start=True, stop=True), "tS")
        etb_mm = c.val("tS")

        @sec("vector")
        def _(e, etb_mm=etb_mm):
            c.wait(e, "tS", etb_mm)
            c.inc(e.tensor_copy(s_smallc[:, 0:1], pb[5][:, 0:1]), "vS")
            c.wait(e, "vS")
            c.inc(e.tensor_scalar(s_relc[:], s_relc[:], s_smallc[:, 0:1], None, op0=OP.mult), "vS")
        relc_done = c.val("vS")

        @sec("tensor")
        def _(e, relc_done=relc_done):
            c.wait(e, "vS", relc_done)
            c.inc(e.transpose(pb[6][0:8, 0:128], s_relc[:], s_idf[:]), "tS")
        relT_mm = c.val("tS")

        @sec("vector")
        def _(e, relT_mm=relT_mm):
            c.wait(e, "tS", relT_mm)
            c.inc(e.tensor_copy(s_relrow[:], pb[6][0:8, 0:128]), "vS")
        relrow_done = c.val("vS")

        @sec("sync")
        def _(e, relrow_done=relrow_done):
            c.wait(e, "vS", relrow_done)
            c.inc(e.dma_start(d_cc2i[0:1, :].rearrange("one (qt q) -> qt (one q)", qt=TT),
                              s_relrow[:]), "rlS", 16)
        rel_sent = c.val("rlS")

        @sec("gpsimd")
        def _(e, rel_sent=rel_sent):
            c.wait(e, "rlS", rel_sent)
            c.inc(e.collective_compute(
                "AllGather", OP.bypass,
                ins=[d_cc2i.ap().opt()], outs=[d_cc2o.ap().opt()],
                replica_groups=[[0, 1, 2, 3], [4, 5, 6, 7]]), "ccS")
        cc2_done = c.val("ccS")

        if debug:
            @sec("sync")
            def _(e, cc2_done=cc2_done):
                c.wait(e, "ccS", cc2_done)
                c.inc(e.dma_start(dbg["rel"][:], d_cc2o[:]), "outS", 16)

        w3.close()
        soft_barrier()

        w4 = ExitStack()
        s_rel4 = [sb(w4, f"s_rel4_{r}", [1, TPC], F32) for r in range(4)]
        s_relb = sb(w4, "s_relb", [128, 4096], F32)
        s_bjunk = sb(w4, "s_bjunk", [128, 4096], BF16)
        s_iot1 = sb(w4, "s_iot1", [1, 128], F32)
        s_trow = sb(w4, "s_trow", [1, 128], F32)
        s_crow = sb(w4, "s_crow", [1, 128], F32)
        s_brw = sb(w4, "s_brw", [1, 128], mybir.dt.uint8)
        s_brw2 = sb(w4, "s_brw2", [1, 128], mybir.dt.uint8)
        s_neg = sb(w4, "s_neg", [1, 128], F32)
        s_pos = sb(w4, "s_pos", [1, 128], F32)

        @sec("vector")
        def _(e):
            c.wait(e, "vS", c.val("vS"))
            c.inc(e.memset(s_neg[:], -3.0e38), "vS")
            c.inc(e.memset(s_pos[:], 3.0e38), "vS")

        @sec("sync")
        def _(e, cc2_done=cc2_done):
            c.wait(e, "ccS", cc2_done)
            for r in range(4):
                c.inc(e.dma_start(s_rel4[r][:], d_cc2o[r:r + 1, :]), "r4S", 16)
        rel4_done = c.val("r4S")

        @sec("gpsimd")
        def _(e):
            c.inc(e.iota(s_iot1[:], [[1, 128]], base=0, channel_multiplier=0,
                         allow_small_or_imprecise_dtypes=True), "ioS")
        iot1_done = c.val("ioS")

        for rseg in range(8):
            r, i0 = rseg // 2, (rseg % 2) * 512
            bank = pb[rseg % 2]
            prev_v = c.val("vS")

            @sec("tensor")
            def _(e, r=r, i0=i0, bank=bank, prev_v=prev_v, rel4_done=rel4_done):
                c.wait(e, "r4S", rel4_done)
                c.wait(e, "vS", prev_v)
                c.inc(e.matmul(bank[:], s_onesf[:], s_rel4[r][0:1, i0:i0 + 512],
                               start=True, stop=True), "tS")
            mmd = c.val("tS")

            @sec("vector")
            def _(e, rseg=rseg, bank=bank, mmd=mmd):
                c.wait(e, "tS", mmd)
                c.inc(e.tensor_copy(s_relb[:, rseg * 512:(rseg + 1) * 512], bank[:]), "vS")
        relb_done = c.val("vS")

        for g in range(NKT):
            bank = pb[2 + (g % 2)]
            prev_v = c.val("vS")

            @sec("tensor")
            def _(e, g=g, bank=bank, prev_v=prev_v, relb_done=relb_done):
                c.wait(e, "vS", max(prev_v, relb_done))
                c.inc(e.transpose(bank[:, 0:128], s_relb[:, g * 128:(g + 1) * 128], s_idf[:]), "tS")
            mmd = c.val("tS")

            @sec("vector")
            def _(e, g=g, bank=bank, mmd=mmd):
                c.wait(e, "tS", mmd)
                c.inc(e.tensor_copy(s_relpc[:, g:g + 1], bank[:, 0:1]), "vS")
        relpc_done = c.val("vS")

        @sec("vector")
        def _(e, relb_done=relb_done):
            c.wait(e, "vS", relb_done)
            c.inc(e.tensor_reduce(s_bis[0:1, 0:1], s_relb[0:1, :], axis=AX.X, op=OP.min), "vS")
            c.inc(e.tensor_reduce(s_bis[0:1, 1:2], s_relb[0:1, :], axis=AX.X, op=OP.max), "vS")
        bis_init = c.val("vS")

        for rnd in range(NROUND):
            @sec("vector")
            def _(e, rnd=rnd):
                c.wait(e, "vS", c.val("vS"))
                if rnd == 0:
                    c.wait(e, "ioS", iot1_done)
                c.inc(e.tensor_tensor(s_bis[0:1, 2:3], s_bis[0:1, 1:2], s_bis[0:1, 0:1],
                                      op=OP.subtract), "vS")
                c.wait(e, "vS")
                c.inc(e.tensor_scalar(s_bis[0:1, 2:3], s_bis[0:1, 2:3], 1.0 / 127.0, None,
                                      op0=OP.mult), "vS")
                c.wait(e, "vS")
                c.inc(e.tensor_scalar(s_trow[:], s_iot1[:], s_bis[0:1, 2:3], s_bis[0:1, 0:1],
                                      op0=OP.mult, op1=OP.add), "vS")
            trow_done = c.val("vS")

            bank = pb[4]
            @sec("tensor")
            def _(e, trow_done=trow_done, bank=bank):
                c.wait(e, "vS", trow_done)
                c.inc(e.transpose(bank[:, 0:1], s_trow[:], s_idf[0:1, 0:1]), "tS")
            tcol_mm = c.val("tS")

            @sec("vector")
            def _(e, tcol_mm=tcol_mm, bank=bank):
                c.wait(e, "tS", tcol_mm)
                c.inc(e.tensor_copy(s_smallc[:, 1:2], bank[:, 0:1]), "vS")
                c.wait(e, "vS")
                c.inc(e.tensor_scalar(s_bjunk[:], s_relb[:], s_smallc[:, 1:2], 0.0,
                                      op0=OP.is_ge, op1=OP.add,
                                      accum_out=s_smallc[:, 2:3]), "vS")
            cnt_done = c.val("vS")

            bank2 = pb[5]
            @sec("tensor")
            def _(e, cnt_done=cnt_done, bank2=bank2):
                c.wait(e, "vS", cnt_done)
                c.inc(e.transpose(bank2[0:1, 0:128], s_smallc[:, 2:3], s_idf[:]), "tS")
            crow_mm = c.val("tS")

            @sec("vector")
            def _(e, crow_mm=crow_mm, bank2=bank2):
                c.wait(e, "tS", crow_mm)
                c.inc(e.tensor_copy(s_crow[:], bank2[0:1, 0:128]), "vS")
                c.wait(e, "vS")
                c.inc(e.tensor_scalar(s_brw[:], s_crow[:], 1024.0, None, op0=OP.is_ge), "vS")
                c.inc(e.tensor_scalar(s_brw2[:], s_crow[:], 1024.0, None, op0=OP.is_lt), "vS")
                c.wait(e, "vS")
                c.inc(e.select(s_crow[:], s_brw[:], s_trow[:], s_neg[:], add_drain=True), "vS")
                c.wait(e, "vS")
                c.inc(e.tensor_reduce(s_bis[0:1, 0:1], s_crow[:], axis=AX.X, op=OP.max), "vS")
                c.wait(e, "vS")
                c.inc(e.select(s_crow[:], s_brw2[:], s_trow[:], s_pos[:], add_drain=True), "vS")
                c.wait(e, "vS")
                c.inc(e.tensor_reduce(s_bis[0:1, 1:2], s_crow[:], axis=AX.X, op=OP.min), "vS")

        thr_done = c.val("vS")
        bankt = pb[6]

        @sec("tensor")
        def _(e, thr_done=thr_done, bankt=bankt):
            c.wait(e, "vS", thr_done)
            c.inc(e.matmul(bankt[:, 0:1], s_onesf[:], s_bis[0:1, 0:1], start=True, stop=True), "tS")
        lob_mm = c.val("tS")

        @sec("vector")
        def _(e, lob_mm=lob_mm, bankt=bankt):
            c.wait(e, "tS", lob_mm)
            c.inc(e.tensor_copy(s_smallc[:, 3:4], bankt[:, 0:1]), "vS")
            c.wait(e, "vS")
            c.wait(e, "vS", relpc_done)
            c.inc(e.tensor_scalar(s_sel[:], s_relpc[:], s_smallc[:, 3:4], None, op0=OP.is_ge), "vS")
        sel_done = c.val("vS")

        if debug:
            @sec("sync")
            def _(e, sel_done=sel_done):
                c.wait(e, "vS", sel_done)
                c.inc(e.dma_start(dbg["sel"][:], s_sel[:]), "outS", 16)

        w4.close()
        soft_barrier()

        w5 = ExitStack()
        s_mask = sb(w5, "s_mask", [128, NKT, 512], U8)
        s_av = sb(w5, "s_av", [128, KF, TPC], BF16)
        s_kh = sb(w5, "s_kh", [128, 2, NKT, 128], BF16)
        s_vh = sb(w5, "s_vh", [128, 2, NKT, 128], BF16)
        s_qh = sb(w5, "s_qh", [128, 2, 512], BF16)
        s_E = sb(w5, "s_E", [128, 2, 512], BF16)
        s_P = sb(w5, "s_P", [128, 3, 512], BF16)
        s_qpos = sb(w5, "s_qpos", [128, 512], F32)
        s_kpos = sb(w5, "s_kpos", [128, NKT], F32)
        s_kp512 = sb(w5, "s_kp512", [128, NKT], F32)
        s_sumr = sb(w5, "s_sumr", [1, 512], F32)
        s_rb = sb(w5, "s_rb", [128, 512], F32)
        s_c32 = sb(w5, "s_c32", [128, 1], F32)

        @sec("vector")
        def _(e):
            c.inc(e.memset(s_c32[:], ESHIFT), "vS")
        c32_done = c.val("vS")

        @sec("gpsimd")
        def _(e):
            c.inc(e.iota(s_kpos[:], [[128, NKT]], base=0, channel_multiplier=1,
                         allow_small_or_imprecise_dtypes=True), "ioS")
        kpos_done = c.val("ioS")

        @sec("vector")
        def _(e, kpos_done=kpos_done):
            c.wait(e, "ioS", kpos_done)
            c.inc(e.tensor_scalar(s_kp512[:], s_kpos[:], 512.5, None, op0=OP.add), "vS")
        kp512_done = c.val("vS")

        @sec("tensor")
        def _(e):
            c.wait(e, "ldS", lds_done)
            c.wait(e, "vS", c.val("vS"))
            c.inc(e.matmul(pb[6][:, 0:1], s_onesf[:], s_meta[0:1, 0:1], start=True, stop=True), "tS")
        q0b_mm = c.val("tS")

        @sec("vector")
        def _(e, q0b_mm=q0b_mm):
            c.wait(e, "tS", q0b_mm)
            c.inc(e.tensor_copy(s_smallc[:, 4:5], pb[6][:, 0:1]), "vS")
        q0b_done = c.val("vS")

        kh_free = [0, 0]
        vh_free = [0, 0]
        qh_free = [0, 0]
        kh_load = {}
        vh_load = {}
        qh_load = {}
        E_free = [0, 0]
        P_free = [0, 0, 0]
        sb_free = [0, 0]

        for qb in range(2):
            @sec("gpsimd")
            def _(e, qb=qb):
                c.wait(e, "vS", c.val("vS"))
                c.inc(e.iota(s_qpos[:], [[1, 512]], base=qb * 512, channel_multiplier=0,
                             allow_small_or_imprecise_dtypes=True), "ioS")
            qio_done = c.val("ioS")

            @sec("vector")
            def _(e, qio_done=qio_done, q0b_done=q0b_done):
                c.wait(e, "ioS", qio_done)
                c.wait(e, "vS", q0b_done)
                c.inc(e.tensor_scalar(s_qpos[:], s_qpos[:], s_smallc[:, 4:5], None, op0=OP.add), "vS")
            qpos_done = c.val("vS")

            @sec("vector")
            def _(e, qpos_done=qpos_done, sel_done=sel_done, kp512_done=kp512_done):
                c.wait(e, "vS", max(qpos_done, sel_done, kp512_done))
                for g in range(NKT):
                    c.inc(e.tensor_scalar(s_E[:, 0, :], s_qpos[:], s_kp512[:, g:g + 1],
                                          s_sel[:, g:g + 1], op0=OP.is_lt, op1=OP.max), "vS")
                    c.wait(e, "vS")
                    c.inc(e.scalar_tensor_tensor(s_mask[:, g, :], s_qpos[:], s_kpos[:, g:g + 1],
                                                 s_E[:, 0, :], op0=OP.is_ge, op1=OP.mult), "vS")
                    c.wait(e, "vS")
            mask_done = c.val("vS")

            for h in range(NH):
                hb = h % 2
                jrow = h // 8
                @sec("sync")
                def _(e, qb=qb, h=h, hb=hb, jrow=jrow, khf=kh_free[hb], vhf=vh_free[hb],
                      qhf=qh_free[hb], cc1_done=cc1_done, q_sent=q_sent):
                    c.wait(e, "ccS", cc1_done)
                    c.wait(e, "qsS", q_sent)
                    c.wait(e, "tS", max(khf, vhf, qhf))
                    for r in range(4):
                        c.inc(e.dma_start(
                            s_kh[:, hb, r * 8:(r + 1) * 8, :].rearrange("p g f -> p (g f)"),
                            d_cc1o[r * CC1R + h * 128:r * CC1R + h * 128 + 128, :]), f"stS{hb}", 16)
                    for r in range(4):
                        c.inc(e.dma_start(
                            s_vh[:, hb, r * 8:(r + 1) * 8, :],
                            d_cc1o[r * CC1R + 4096 + jrow * 1024:
                                   r * CC1R + 4096 + jrow * 1024 + 1024,
                                   (h % 8) * 128:(h % 8) * 128 + 128]
                            .rearrange("(g p) f -> p g f", p=128)), f"stS{hb}", 16)
                    c.inc(e.dma_start(s_qh[:, hb, :],
                                      d_qsp[h * 128:(h + 1) * 128, qb * 512:(qb + 1) * 512]), f"stS{hb}", 16)
                    kh_load[(qb, h)] = (f"stS{hb}", c.val(f"stS{hb}"))

                pav = pb[2 + (h % 2)]
                psums = pb[4 + (h % 2)]
                for g in range(NKT):
                    sbank = pb[g % 2]

                    @sec("tensor")
                    def _(e, qb=qb, h=h, hb=hb, g=g, sbank=sbank, pav=pav, psums=psums,
                          sbf=sb_free[g % 2]):
                        if g == 0:
                            c.wait(e, *kh_load[(qb, h)])
                            c.wait(e, "vS", mask_done)
                        c.wait(e, "aS", sbf)
                        c.inc(e.matmul(sbank[:], s_kh[:, hb, g, :], s_qh[:, hb, :],
                                       start=True, stop=True), "tS")
                    smm = c.val("tS")

                    @sec("scalar")
                    def _(e, g=g, sbank=sbank, smm=smm, ef=max(E_free[g % 2], c32_done)):
                        c.wait(e, "tS", smm)
                        c.wait(e, "vS", ef)
                        c.inc(e.activation(s_E[:, g % 2, :], sbank[:], AF.Exp,
                                           bias=s_c32[:], scale=SCALE), "aS")
                    sb_free[g % 2] = c.val("aS")
                    e_done = c.val("aS")

                    @sec("vector")
                    def _(e, qb=qb, g=g, e_done=e_done, pf=P_free[g % 3]):
                        c.wait(e, "aS", e_done)
                        c.wait(e, "tS", pf)
                        c.inc(e.tensor_tensor(s_P[:, g % 3, :], s_E[:, g % 2, :],
                                              s_mask[:, g, :], op=OP.mult), "vS")
                    E_free[g % 2] = c.val("vS")
                    p_done = c.val("vS")

                    @sec("tensor")
                    def _(e, qb=qb, h=h, hb=hb, g=g, pav=pav, psums=psums, p_done=p_done):
                        c.wait(e, "vS", p_done)
                        if g == 0:
                            c.wait(e, *kh_load[(qb, h)])
                        c.inc(e.matmul(pav[:], s_vh[:, hb, g, :], s_P[:, g % 3, :],
                                       start=(g == 0), stop=(g == NKT - 1)), "tS")
                        c.inc(e.matmul(psums[0:1, :], s_onesbf[:, 0:1], s_P[:, g % 3, :],
                                       start=(g == 0), stop=(g == NKT - 1)), "tS")
                        if g == NKT - 1:
                            vh_free[hb] = c.val("tS")
                    P_free[g % 3] = c.val("tS")
                    if g == NKT - 1:
                        kh_free[hb] = c.val("tS")
                        qh_free[hb] = c.val("tS")
                av_mm_done = c.val("tS")

                @sec("vector")
                def _(e, h=h, psums=psums, av_mm_done=av_mm_done):
                    c.wait(e, "tS", av_mm_done)
                    c.inc(e.tensor_copy(s_sumr[:], psums[0:1, :]), "vS")
                    c.wait(e, "vS")
                    c.inc(e.reciprocal(s_sumr[:], s_sumr[:]), "vS")
                sum_done = c.val("vS")

                @sec("tensor")
                def _(e, sum_done=sum_done):
                    c.wait(e, "vS", sum_done)
                    c.inc(e.matmul(pb[6][:], s_onesf[:], s_sumr[:], start=True, stop=True), "tS")
                rb_mm = c.val("tS")

                @sec("vector")
                def _(e, qb=qb, h=h, pav=pav, rb_mm=rb_mm):
                    c.wait(e, "tS", rb_mm)
                    c.inc(e.tensor_copy(s_rb[:], pb[6][:]), "vS")
                    c.wait(e, "vS")
                    c.inc(e.tensor_tensor(s_av[:, h, qb * 512:(qb + 1) * 512], pav[:],
                                          s_rb[:], op=OP.mult), "vS")
        av_done = c.val("vS")

        def wo_evac(e, tt, coln, psum):
            c.inc(e.tensor_tensor(s_x[:, tt, coln * 512:(coln + 1) * 512], psum[:],
                                  s_x[:, tt, coln * 512:(coln + 1) * 512], op=OP.add), "vS")

        x2_done = tm_sweep("wo", d_Wo, s_av, av_done, wo_evac)

        if debug:
            @sec("sync")
            def _(e, x2_done=x2_done):
                c.wait(e, "vS", x2_done)
                c.inc(e.dma_start(dbg["x2"].rearrange("(tt p) f -> p tt f", p=128), s_x[:]), "outS", 16)

        w5.close()
        soft_barrier()

        w6 = ExitStack()
        fb2 = open_ffn_window(w6)
        xnT3_done = layernorm_to_xnT(fb2["xnT"], fb2["sq_junk"],
                                     xwait=("vS", x2_done), with_logits=True)
        ce2_done = top2()
        x3_done = moe(fb2, ce2_done, xnT3_done)
        w6.close()

        @sec("sync")
        def _(e):
            c.wait(e, "vS", c.val("vS"))
            c.wait(e, "aS", c.val("aS"))
            c.wait(e, "tS", c.val("tS"))
            c.inc(e.dma_start(d_out.rearrange("(tt p) f -> p tt f", p=128), s_x[:]), "outS", 16)
            c.wait(e, "outS", c.val("outS"))

    return nc, dbg



def make_in_maps(inputs):
    x_full = np.ascontiguousarray(inputs["L0_attn_hidden_states"], dtype=np.float32)
    ident = np.eye(128, dtype=np.float32)
    consts = dict(
        id_bf=ident.astype(ml_dtypes.bfloat16),
        id_f32=ident,
        ones_bf=np.ones((128, 128), dtype=np.float32).astype(ml_dtypes.bfloat16),
        ones_f32=np.ones((1, 128), dtype=np.float32),
        halves_bf=np.full((1, 128), 0.5, dtype=np.float32).astype(ml_dtypes.bfloat16),
        Wq=inputs["L0_attn_Wq"], Wk=inputs["L0_attn_Wk"], Wv=inputs["L0_attn_Wv"],
        Wo=inputs["L0_attn_Wo"], Wq_ind=inputs["L0_attn_Wq_ind"], Wk_ind=inputs["L0_attn_Wk_ind"],
        head_w=np.asarray(inputs["L0_attn_head_weights"], dtype=np.float32).reshape(1, 8),
        temp=np.asarray(inputs["L0_attn_temperature_param"], dtype=np.float32).reshape(1, 1),
        router_w=inputs["L0_ffn_router_w"],
        gate_w=inputs["L0_ffn_gate_w"], up_w=inputs["L0_ffn_up_w"], down_w=inputs["L0_ffn_down_w"],
        s_gate_w=inputs["L0_ffn_s_gate_w"], s_up_w=inputs["L0_ffn_s_up_w"],
        s_down_w=inputs["L0_ffn_s_down_w"],
        s_gate_b=np.asarray(inputs["L0_ffn_s_gate_b"], dtype=np.float32).reshape(1, SI),
        s_up_b=np.asarray(inputs["L0_ffn_s_up_b"], dtype=np.float32).reshape(1, SI),
        s_down_b=np.asarray(inputs["L0_ffn_s_down_b"], dtype=np.float32).reshape(1, H),
    )
    out = {}
    for k, v in consts.items():
        v = np.ascontiguousarray(v)
        if v.dtype != ml_dtypes.bfloat16:
            v = v.astype(np.float32, copy=False)
        out[k] = v
    in_maps = []
    for core in range(NCORES):
        b, sl = core // 4, core % 4
        meta = np.zeros((1, 8), dtype=np.float32)
        meta[0, 0] = sl * TPC
        meta[0, 1] = float(b)
        m = dict(out)
        m["x_in"] = np.ascontiguousarray(x_full[b, sl * TPC:(sl + 1) * TPC, :])
        m["meta"] = meta
        in_maps.append(m)
    return in_maps


def kernel(**inputs):
    nc, _ = build_program(debug=False)
    in_maps = make_in_maps(inputs)
    res = run_bass_kernel_spmd(nc, in_maps, core_ids=list(range(NCORES)))
    out = np.empty((B, S, H), dtype=np.float32)
    for core in range(NCORES):
        b, sl = core // 4, core % 4
        out[b, sl * TPC:(sl + 1) * TPC, :] = res.results[core]["x_out"]
    return out
